# version 04 (final)
# baseline (speedup 1.0000x reference)
import numpy as np
import ml_dtypes
import concourse.bass as bass
import concourse.mybir as mybir
from concourse.bass_utils import run_bass_kernel_spmd

F32 = mybir.dt.float32
BF16 = mybir.dt.bfloat16
I32 = mybir.dt.int32
AF = mybir.ActivationFunctionType
ALU = mybir.AluOpType
AX = mybir.AxisListType
bf = ml_dtypes.bfloat16

NCORES = 8
TCORE = 2048
D = 1024
H = 3072
C = 512
NSUB = TCORE // C
NGT = TCORE // 128
ALPHA = 0.99
ALPHA_C = float(np.float32(ALPHA ** np.float64(C)))
S127 = 1.0 / 127.0
INVSQ = 1.0 / 32.0

ENGS = ("tensor", "vector", "scalar", "sync", "gpsimd")


class Plan:
    def __init__(self):
        self.ev = {e: [] for e in ENGS}
        self.c = {"dS": 0, "dG": 0, "dC": 0, "tP": 0, "vP": 0, "aP": 0, "cc": 0}

    def wait(self, eng, sem, val):
        if val and val > 0:
            self.ev[eng].append(("w", sem, val))

    def run(self, eng, fn, incs=()):
        self.ev[eng].append(("r", fn, tuple(incs)))
        for s, n in incs:
            self.c[s] += n

    def dma(self, eng, out, in_, sem):
        self.run(eng, lambda h, o=out, i=in_: h.dma_start(out=o, in_=i),
                 [(sem, 16)])
        return self.c[sem]

    def mm(self, out, lhsT, rhs, start, stop, inc=None):
        incs = [(inc, 1)] if inc else []
        self.run("tensor",
                 lambda h, o=out, l=lhsT, r=rhs, st=start, sp=stop:
                 h.matmul(o, l, r, start=st, stop=sp), incs)
        return self.c[inc] if inc else None

    def tt(self, op, out, in0, in1, inc=None):
        incs = [(inc, 1)] if inc else []
        self.run("vector",
                 lambda h, o=out, a=in0, b=in1, q=op: h.tensor_tensor(o, a, b, op=q),
                 incs)
        return self.c[inc] if inc else None

    def stt(self, out, in0, scalar, in1, op0, op1, inc=None):
        incs = [(inc, 1)] if inc else []
        self.run("vector",
                 lambda h, o=out, a=in0, s=scalar, b=in1, p0=op0, p1=op1:
                 h.scalar_tensor_tensor(o, a, s, b, p0, p1), incs)
        return self.c[inc] if inc else None

    def act(self, out, in_, func, bias=0.0, scale=1.0, inc=None):
        incs = [(inc, 1)] if inc else []
        self.run("scalar",
                 lambda h, o=out, i=in_, f=func, b=bias, s=scale:
                 h.activation(out=o, in_=i, func=f, bias=b, scale=s), incs)
        return self.c[inc] if inc else None


def build_nc(debug=False):
    nc = bass.Bass("TRN2", target_bir_lowering=False)

    x_in = nc.dram_tensor("x_in", [TCORE, D], F32, kind="ExternalInput").ap()
    wq_sl = nc.dram_tensor("wq_sl", [128, 3 * D], F32, kind="ExternalInput").ap()
    wg_sl = nc.dram_tensor("wg_sl", [128, 3 * D], I32, kind="ExternalInput").ap()
    wu_sl = nc.dram_tensor("wu_sl", [128, 3 * D], I32, kind="ExternalInput").ap()
    wd_sl = nc.dram_tensor("wd_sl", [128, 3 * D], I32, kind="ExternalInput").ap()
    bpp_in = nc.dram_tensor("bpp_in", [128, 24], F32, kind="ExternalInput").ap()
    bbc_in = nc.dram_tensor("bbc_in", [128, 2 * D], F32, kind="ExternalInput").ap()
    qdec_in = nc.dram_tensor("qdec_in", [128, TCORE], F32, kind="ExternalInput").ap()
    kdec_in = nc.dram_tensor("kdec_in", [128, TCORE], F32, kind="ExternalInput").ap()
    kpp_in = nc.dram_tensor("kpp_in", [128, NGT], F32, kind="ExternalInput").ap()
    relay_in = nc.dram_tensor("relay_in", [128, 8], F32, kind="ExternalInput").ap()
    mask_in = nc.dram_tensor("mask_in", [128, 4 * C], BF16, kind="ExternalInput").ap()
    ident_in = nc.dram_tensor("ident_in", [128, 128], BF16, kind="ExternalInput").ap()
    out_ext = nc.dram_tensor("out_ext", [TCORE, D], F32, kind="ExternalOutput").ap()

    ccw_in_q = nc.dram_tensor("ccw_in_q", [128, 3 * D], BF16).ap()
    ccw_out_q = nc.dram_tensor("ccw_out_q", [D, 3 * D], BF16).ap()
    ccw_in_g = nc.dram_tensor("ccw_in_g", [128, H], BF16).ap()
    ccw_out_g = nc.dram_tensor("ccw_out_g", [D, H], BF16).ap()
    ccw_in_u = nc.dram_tensor("ccw_in_u", [128, H], BF16).ap()
    ccw_out_u = nc.dram_tensor("ccw_out_u", [D, H], BF16).ap()
    ccw_in_d = nc.dram_tensor("ccw_in_d", [384, D], BF16).ap()
    ccw_out_d = nc.dram_tensor("ccw_out_d", [H, D], BF16).ap()
    scr_qd = nc.dram_tensor("scr_qd", [NSUB * 8 * 128, C], BF16).ap()
    scr_sl = nc.dram_tensor("scr_sl", [TCORE, D], BF16).ap()
    scr_gg = nc.dram_tensor("scr_gg", [TCORE, D], BF16).ap()
    cc_in = nc.dram_tensor("cc_in", [D, D], BF16).ap()
    cc_out = nc.dram_tensor("cc_out", [NCORES * D, D], BF16).ap()
    bar1_in = nc.dram_tensor("bar1_in", [128, 16], BF16).ap()
    bar1_out = nc.dram_tensor("bar1_out", [NCORES * 128, 16], BF16).ap()
    bar2_in = nc.dram_tensor("bar2_in", [128, 16], BF16).ap()
    bar2_out = nc.dram_tensor("bar2_out", [NCORES * 128, 16], BF16).ap()
    if debug:
        dbg_qd = nc.dram_tensor("dbg_qd", [NSUB * 8 * 128, C], BF16,
                                kind="ExternalOutput").ap()
        dbg_sl = nc.dram_tensor("dbg_sl", [TCORE, D], BF16, kind="ExternalOutput").ap()
        dbg_gg = nc.dram_tensor("dbg_gg", [TCORE, D], BF16, kind="ExternalOutput").ap()
        dbg_ccin = nc.dram_tensor("dbg_ccin", [D, D], BF16, kind="ExternalOutput").ap()
        dbg_wq = nc.dram_tensor("dbg_wq", [D, 3 * D], BF16, kind="ExternalOutput").ap()
        dbg_wd = nc.dram_tensor("dbg_wd", [H, D], BF16, kind="ExternalOutput").ap()

    from contextlib import ExitStack
    with ExitStack() as stack:
        NBF = 74368
        arb_h = stack.enter_context(nc.sbuf_tensor("arb", [128, NBF], BF16))
        arb = arb_h[:, :]
        NF3 = 15536
        arf_h = stack.enter_context(nc.sbuf_tensor("arf", [128, NF3], F32))
        arf = arf_h[:, :]
        psum_h = stack.enter_context(nc.psum_tensor("ps", [128, 8 * 512], F32))
        psum = psum_h[:, :]

        def pb(k):
            return psum[:, 512 * k:512 * (k + 1)]

        ofs_b = [0]

        def ab(n, sh=None):
            o = ofs_b[0]
            ofs_b[0] += (n + 15) // 16 * 16
            assert ofs_b[0] <= NBF, f"bf16 arena overflow {ofs_b[0]}"
            flat = arb[:, o:o + n]
            if sh is None:
                return flat
            return flat.rearrange("p (a b) -> p a b", a=sh[0]), flat

        ofs_f = [0]

        def af(n, sh=None):
            o = ofs_f[0]
            ofs_f[0] += (n + 15) // 16 * 16
            assert ofs_f[0] <= NF3, f"f32 arena overflow {ofs_f[0]}"
            flat = arf[:, o:o + n]
            if sh is None:
                return flat
            return flat.rearrange("p (a b) -> p a b", a=sh[0]), flat

        xnT, _ = ab(8 * TCORE, (8, TCORE))
        ident = ab(128)
        breg = ofs_b[0]
        xn_tile, _ = ab(4 * D, (4, D))
        stage, _ = ab(4 * 3072, (4, 3072))
        contrib, _ = ab(2 * 3072, (2, 3072))
        ofs_b[0] = breg
        qdecT, _ = ab(8 * C, (8, C))
        kfmT, _ = ab(8 * C, (8, C))
        ktm, _ = ab(4 * D, (4, D))
        v_s, _ = ab(4 * D, (4, D))
        Bsc, _ = ab(4 * C, (4, C))
        sl_s, _ = ab(2 * D, (2, D))
        S_b, _ = ab(8 * D, (8, D))
        wqkvT, _ = ab(8 * 3 * D, (8, 3 * D))
        maskb, maskb_flat = ab(4 * C, (4, C))
        qdx, _ = ab(2 * C, (2, C))
        p12_end = ofs_b[0]
        ofs_b[0] = breg
        hT, _ = ab(24 * C, (24, C))
        wgu, _ = ab(2 * 8 * D, (2, 8 * D))
        wdc, _ = ab(24 * C, (24, C))
        gsil, _ = ab(4 * C, (4, C))
        g_buf, _ = ab(4 * D, (4, D))
        p3_end = ofs_b[0]
        ofs_b[0] = breg
        acc, _ = ab(8 * D, (8, D))
        dj, _ = ab(2 * D, (2, D))
        qd4, _ = ab(2 * 8 * C, (2, 8 * C))
        sl4, _ = ab(2 * D, (2, D))
        gb4, _ = ab(2 * D, (2, D))
        p4_end = ofs_b[0]
        ofs_b[0] = max(p12_end, p3_end, p4_end)

        def wgu_sl(pg, d, a, b):
            return wgu[:, pg, d * D + a:d * D + b]

        def qd4_sl(gp, dk, a, b):
            return qd4[:, gp, dk * C + a:dk * C + b]

        bpp = af(24)
        kpp = af(NGT)
        relay = af(8)
        tmpA = af(512)
        tmpB = af(512)
        st1, _ = af(2 * 8, (2, 8))
        fregB = ofs_f[0]
        xbuf, _ = af(4 * D, (4, D))
        xsq, _ = af(2 * D, (2, D))
        pB_end = ofs_f[0]
        ofs_f[0] = fregB
        qdec_c = af(TCORE)
        kdec_c = af(TCORE)
        bbc = af(2 * D)
        ofs_f[0] = max(pB_end, ofs_f[0])
        fregC = ofs_f[0]
        S, S_flat = af(8 * D, (8, D))
        pC_end = ofs_f[0]
        ofs_f[0] = fregC
        x4, _ = af(2 * D, (2, D))
        o4, _ = af(2 * D, (2, D))
        t24, _ = af(2 * 512, (2, 512))
        ofs_f[0] = max(pC_end, ofs_f[0])

        sems = {}
        for sname in ("dS", "dG", "dC", "tP", "vP", "aP", "cc"):
            sems[sname] = stack.enter_context(nc.semaphore(sname))
        block = stack.enter_context(nc.Block())

        p = Plan()
        bank_free = [("tP", 0)] * 8
        bank_rr = [0]

        def next_bank():
            k = bank_rr[0] % 8
            bank_rr[0] += 1
            sem, val = bank_free[k]
            p.wait("tensor", sem, val)
            return k

        p.dma("sync", bpp, bpp_in, "dS")
        p.dma("sync", kpp, kpp_in, "dS")
        p.dma("sync", relay, relay_in, "dS")
        p.dma("sync", ident, ident_in, "dS")
        dS_small = p.c["dS"]
        p.wait("tensor", "dS", dS_small)
        p.wait("scalar", "dS", dS_small)

        prep = [(wq_sl, ccw_in_q, 128),
                (wg_sl, ccw_in_g, 128),
                (wu_sl, ccw_in_u, 128),
                (wd_sl, ccw_in_d, 384)]
        cast_done = [0] * 4
        for i, (src, dst, w_) in enumerate(prep):
            cast_done[i] = p.dma("gpsimd", stage[:, i, :], src, "dG")
        contrib_free = [("dG", 0), ("dG", 0)]
        dG_w = [0] * 4
        for i, (src, dst, w_) in enumerate(prep):
            pr = i % 2
            nj = w_ // 128
            na = 3072 // w_
            p.wait("tensor", "dG", cast_done[i])
            sem, val = contrib_free[pr]
            p.wait("vector", sem, val)
            for blk in range(24):
                a_, jc = blk // nj, blk % nj
                dst_off = jc * (na * 128) + a_ * 128
                k = next_bank()
                p.run("tensor", lambda h, o=pb(k)[:, 0:128],
                      ii=stage[:, i, blk * 128:(blk + 1) * 128], idn=ident:
                      h.matmul(o, ii, idn, start=True, stop=True), [("tP", 1)])
                tv = p.c["tP"]
                p.wait("vector", "tP", tv)
                p.run("vector", lambda h, o=contrib[:, pr, dst_off:dst_off + 128],
                      ii=pb(k)[:, 0:128]: h.tensor_copy(out=o, in_=ii), [("vP", 1)])
                bank_free[k] = ("vP", p.c["vP"])
            p.wait("gpsimd", "vP", p.c["vP"])
            if w_ == 128:
                dG_w[i] = p.dma("gpsimd", dst[:, :], contrib[:, pr, :], "dG")
            else:
                for j in range(3):
                    dG_w[i] = p.dma("gpsimd", dst[j * 128:(j + 1) * 128, :],
                                    contrib[:, pr, j * 1024:(j + 1) * 1024], "dG")
            contrib_free[pr] = ("dG", dG_w[i])
        p.wait("gpsimd", "dG", dG_w[3])
        cc_w = [0] * 4
        for i, (cin, cout) in enumerate([(ccw_in_q, ccw_out_q), (ccw_in_g, ccw_out_g),
                                         (ccw_in_u, ccw_out_u), (ccw_in_d, ccw_out_d)]):
            p.run("gpsimd", lambda h, a=cin, b=cout: h.collective_compute(
                "AllGather", ALU.bypass, replica_groups=[list(range(NCORES))],
                ins=[a[:, :]], outs=[b[:, :]]), [("cc", 1)])
            cc_w[i] = p.c["cc"]
        p.run("gpsimd", lambda h: h.collective_compute(
            "AllGather", ALU.bypass, replica_groups=[list(range(NCORES))],
            ins=[bar1_in[:, :]], outs=[bar1_out[:, :]]), [("cc", 1)])
        cc_bar1 = p.c["cc"]

        p.wait("sync", "dG", cast_done[3])
        p.wait("sync", "cc", cc_bar1)

        ap_xn_done = [0] * NGT
        xsq_free = [0, 0]
        for gt in range(NGT):
            pr, p2 = gt % 4, gt % 2
            if gt >= 4:
                p.wait("sync", "aP", ap_xn_done[gt - 4])
            dv = p.dma("sync", xbuf[:, pr, :], x_in[gt * 128:(gt + 1) * 128, :], "dS")
            p.wait("vector", "dS", dv)
            p.run("vector", lambda h, o=st1[:, p2, 0:1], i=xbuf[:, pr, :]:
                  h.reduce_sum(out=o, in_=i, axis=AX.X))
            p.wait("scalar", "dS", dv)
            if gt >= 2:
                p.wait("scalar", "vP", xsq_free[p2])
            av = p.act(xsq[:, p2, :], xbuf[:, pr, :], AF.Square, inc="aP")
            p.wait("vector", "aP", av)
            xsq_free[p2] = p.c["vP"] + 1
            p.run("vector", lambda h, o=st1[:, p2, 1:2], i=xsq[:, p2, :]:
                  h.reduce_sum(out=o, in_=i, axis=AX.X), [("vP", 1)])
            p.run("vector", lambda h, o=st1[:, p2, 2:3], i=st1[:, p2, 0:1]:
                  h.tensor_scalar_mul(o, i, 1.0 / D))
            p.run("vector", lambda h, o=st1[:, p2, 3:4], i=st1[:, p2, 1:2]:
                  h.tensor_scalar_mul(o, i, 1.0 / D))
            p.tt(ALU.mult, st1[:, p2, 4:5], st1[:, p2, 2:3], st1[:, p2, 2:3])
            p.tt(ALU.subtract, st1[:, p2, 5:6], st1[:, p2, 3:4], st1[:, p2, 4:5])
            p.run("vector", lambda h, o=st1[:, p2, 5:6]:
                  h.tensor_scalar_add(o, o, 1e-5))
            vv = p.c["vP"] + 1
            p.run("vector", lambda h, o=st1[:, p2, 6:7], i=st1[:, p2, 5:6]:
                  h.reciprocal(out=o, in_=i), [("vP", 1)])
            p.wait("scalar", "vP", vv)
            av = p.act(st1[:, p2, 6:7], st1[:, p2, 6:7], AF.Sqrt, inc="aP")
            p.wait("vector", "aP", av)
            vv = p.stt(st1[:, p2, 7:8], st1[:, p2, 2:3], -1.0, st1[:, p2, 6:7],
                       ALU.mult, ALU.mult, inc="vP")
            p.wait("scalar", "vP", vv)
            ap_xn_done[gt] = p.act(xn_tile[:, pr, :], xbuf[:, pr, :], AF.Identity,
                                   bias=st1[:, p2, 7:8], scale=st1[:, p2, 6:7],
                                   inc="aP")
            p.wait("tensor", "aP", ap_xn_done[gt])
            for d in range(8):
                k = next_bank()
                p.run("tensor", lambda h, o=pb(k)[:, 0:128],
                      ii=xn_tile[:, pr, d * 128:(d + 1) * 128], idn=ident:
                      h.matmul(o, ii, idn, start=True, stop=True), [("tP", 1)])
                tv = p.c["tP"]
                p.wait("scalar", "tP", tv)
                av = p.act(xnT[:, d, gt * 128:(gt + 1) * 128], pb(k)[:, 0:128],
                           AF.Copy, inc="aP")
                bank_free[k] = ("aP", av)

        xnT_done = p.c["aP"]

        p.wait("sync", "cc", cc_w[0])
        for d in range(8):
            p.dma("sync", wqkvT[:, d, :], ccw_out_q[d * 128:(d + 1) * 128, :], "dS")
        dS_wqkv = p.c["dS"]

        p.wait("sync", "aP", ap_xn_done[NGT - 1])
        p.dma("sync", qdec_c, qdec_in, "dS")
        p.dma("sync", kdec_c, kdec_in, "dS")
        p.dma("sync", bbc, bbc_in, "dS")
        dS_consts = p.c["dS"]
        p.dma("sync", maskb_flat, mask_in, "dS")
        dS_mask = p.c["dS"]

        p.run("vector", lambda h, o=S_flat: h.memset(o, 0.0), [("vP", 1)])
        vS0 = p.c["vP"]

        p.wait("tensor", "dS", max(dS_wqkv, dS_consts, dS_mask))
        p.wait("tensor", "aP", xnT_done)
        p.wait("vector", "dG", dG_w[3])
        p.wait("scalar", "dG", dG_w[3])
        p.wait("vector", "dS", dS_mask)
        p.wait("vector", "vP", vS0)

        tmp_free = {0: ("tP", 0), 1: ("tP", 0)}
        tmps = [tmpA, tmpB]
        tpar = [0]

        sb_done = [0] * NSUB
        dC_sl = [0, 0]
        dC_qdx = [0, 0]

        for s in range(NSUB):
            t0 = C * s
            for f in range(8):
                k = next_bank()
                for d in range(8):
                    p.mm(pb(k), wqkvT[:, d, f * 128:(f + 1) * 128],
                         xnT[:, d, t0:t0 + C], start=(d == 0), stop=(d == 7),
                         inc="tP" if d == 7 else None)
                tv = p.c["tP"]
                tp = tpar[0] % 2
                tpar[0] += 1
                sem, val = tmp_free[tp]
                p.wait("scalar", sem, val)
                p.wait("scalar", "tP", tv)
                av = p.act(tmps[tp], pb(k), AF.Identity, bias=bpp[:, f:f + 1],
                           inc="aP")
                p.wait("vector", "aP", av)
                vv = p.tt(ALU.mult, qdecT[:, f, :], tmps[tp], qdec_c[:, t0:t0 + C],
                          inc="vP")
                tmp_free[tp] = ("vP", vv)
                p.wait("vector", "dC", dC_qdx[f % 2])
                vv = p.run("vector", lambda h, o=qdx[:, f % 2, :], i=qdecT[:, f, :],
                           sc=float(np.float32(ALPHA ** np.float64(s * C))):
                           h.tensor_scalar_mul(o, i, sc), [("vP", 1)]) or p.c["vP"]
                bank_free[k] = ("aP", av)
                p.wait("scalar", "vP", vv)
                r0 = (s * 8 + f) * 128
                dC_qdx[f % 2] = p.dma("scalar", scr_qd[r0:r0 + 128, :],
                                      qdx[:, f % 2, :], "dC")
            vP_qdec = p.c["vP"]
            for f in range(8):
                k = next_bank()
                for d in range(8):
                    p.mm(pb(k), wqkvT[:, d, D + f * 128:D + (f + 1) * 128],
                         xnT[:, d, t0:t0 + C], start=(d == 0), stop=(d == 7),
                         inc="tP" if d == 7 else None)
                tv = p.c["tP"]
                tp = tpar[0] % 2
                tpar[0] += 1
                sem, val = tmp_free[tp]
                p.wait("scalar", sem, val)
                p.wait("scalar", "tP", tv)
                av = p.act(tmps[tp], pb(k), AF.Identity, bias=bpp[:, 8 + f:9 + f],
                           inc="aP")
                p.wait("vector", "aP", av)
                vv = p.tt(ALU.mult, kfmT[:, f, :], tmps[tp], kdec_c[:, t0:t0 + C],
                          inc="vP")
                tmp_free[tp] = ("vP", vv)
                bank_free[k] = ("aP", av)
            vP_kfm = p.c["vP"]
            for tt in range(4):
                for hh in range(2):
                    k = next_bank()
                    for d in range(8):
                        p.mm(pb(k), xnT[:, d, t0 + tt * 128:t0 + (tt + 1) * 128],
                             wqkvT[:, d, D + hh * 512:D + (hh + 1) * 512],
                             start=(d == 0), stop=(d == 7),
                             inc="tP" if d == 7 else None)
                    tv = p.c["tP"]
                    tp = tpar[0] % 2
                    tpar[0] += 1
                    sem, val = tmp_free[tp]
                    p.wait("vector", sem, val)
                    p.wait("vector", "tP", tv)
                    vv = p.tt(ALU.add, tmps[tp], pb(k), bbc[:, hh * 512:(hh + 1) * 512],
                              inc="vP")
                    p.wait("scalar", "vP", vv)
                    av = p.act(ktm[:, tt, hh * 512:(hh + 1) * 512], tmps[tp],
                               AF.Identity, scale=kpp[:, 4 * s + tt:4 * s + tt + 1],
                               inc="aP")
                    tmp_free[tp] = ("aP", av)
                    bank_free[k] = ("vP", vv)
            aP_ktm = p.c["aP"]
            for tt in range(4):
                for hh in range(2):
                    k = next_bank()
                    for d in range(8):
                        p.mm(pb(k), xnT[:, d, t0 + tt * 128:t0 + (tt + 1) * 128],
                             wqkvT[:, d, 2 * D + hh * 512:2 * D + (hh + 1) * 512],
                             start=(d == 0), stop=(d == 7),
                             inc="tP" if d == 7 else None)
                    tv = p.c["tP"]
                    p.wait("vector", "tP", tv)
                    vv = p.tt(ALU.add, v_s[:, tt, hh * 512:(hh + 1) * 512], pb(k),
                              bbc[:, D + hh * 512:D + (hh + 1) * 512], inc="vP")
                    bank_free[k] = ("vP", vv)
            vP_v = p.c["vP"]
            p.wait("tensor", "vP", max(vP_qdec, vP_kfm))
            for b in range(4):
                k = next_bank()
                for d in range(8):
                    p.mm(pb(k), kfmT[:, d, b * 128:(b + 1) * 128], qdecT[:, d, :],
                         start=(d == 0), stop=(d == 7), inc="tP" if d == 7 else None)
                tv = p.c["tP"]
                p.wait("vector", "tP", tv)
                vv = p.tt(ALU.mult, Bsc[:, b, :], pb(k), maskb[:, b, :], inc="vP")
                bank_free[k] = ("vP", vv)
            vP_B = p.c["vP"]
            p.wait("tensor", "vP", max(vP_B, vP_v))
            if s > 0:
                p.wait("tensor", "aP", sb_done[s - 1])
            for i in range(4):
                for hh in range(2):
                    k = next_bank()
                    nmm = 4 + (8 if s > 0 else 0)
                    cnt = 0
                    tv = None
                    for b in range(4):
                        cnt += 1
                        tv = p.mm(pb(k), Bsc[:, b, i * 128:(i + 1) * 128],
                                  v_s[:, b, hh * 512:(hh + 1) * 512],
                                  start=(cnt == 1), stop=(cnt == nmm),
                                  inc="tP" if cnt == nmm else None) or tv
                    if s > 0:
                        for d in range(8):
                            cnt += 1
                            tv = p.mm(pb(k), qdecT[:, d, i * 128:(i + 1) * 128],
                                      S_b[:, d, hh * 512:(hh + 1) * 512],
                                      start=False, stop=(cnt == nmm),
                                      inc="tP" if cnt == nmm else None) or tv
                    p.wait("scalar", "tP", tv)
                    p.wait("scalar", "dC", dC_sl[i % 2])
                    av = p.act(sl_s[:, i % 2, hh * 512:(hh + 1) * 512], pb(k),
                               AF.Copy, inc="aP")
                    bank_free[k] = ("aP", av)
                    if hh == 1:
                        r0 = t0 + i * 128
                        dC_sl[i % 2] = p.dma("scalar", scr_sl[r0:r0 + 128, :],
                                             sl_s[:, i % 2, :], "dC")
            p.wait("tensor", "aP", aP_ktm)
            for dk in range(8):
                for hh in range(2):
                    k = next_bank()
                    for b in range(4):
                        p.mm(pb(k), ktm[:, b, dk * 128:(dk + 1) * 128],
                             v_s[:, b, hh * 512:(hh + 1) * 512],
                             start=(b == 0), stop=(b == 3),
                             inc="tP" if b == 3 else None)
                    tv = p.c["tP"]
                    p.wait("vector", "tP", tv)
                    vv = p.stt(S[:, dk, hh * 512:(hh + 1) * 512],
                               S[:, dk, hh * 512:(hh + 1) * 512], ALPHA_C, pb(k),
                               ALU.mult, ALU.add, inc="vP")
                    bank_free[k] = ("vP", vv)
                    p.wait("scalar", "vP", vv)
                    p.act(S_b[:, dk, hh * 512:(hh + 1) * 512],
                          S[:, dk, hh * 512:(hh + 1) * 512], AF.Copy, inc="aP")
            sb_done[s] = p.c["aP"]

        tP_last_ret = p.c["tP"]
        dC_qd_all = max(dC_qdx)
        dC_sl_all = max(dC_sl)

        p.wait("gpsimd", "aP", sb_done[NSUB - 1])
        for dk in range(8):
            p.dma("gpsimd", cc_in[dk * 128:(dk + 1) * 128, :], S_b[:, dk, :], "dG")
        dG_export = p.c["dG"]
        p.wait("gpsimd", "dG", dG_export)
        p.run("gpsimd", lambda h: h.collective_compute(
            "AllGather", ALU.bypass, replica_groups=[list(range(NCORES))],
            ins=[cc_in[:, :]], outs=[cc_out[:, :]]), [("cc", 1)])
        p.run("gpsimd", lambda h: h.collective_compute(
            "AllGather", ALU.bypass, replica_groups=[list(range(NCORES))],
            ins=[bar2_in[:, :]], outs=[bar2_out[:, :]]), [("cc", 1)])
        cc_state = p.c["cc"]

        p.wait("sync", "dG", dG_export)
        p.wait("sync", "cc", cc_state)
        p.wait("sync", "tP", tP_last_ret)
        p.wait("sync", "dC", max(dC_qd_all, dC_sl_all))
        p.wait("vector", "dC", dC_qd_all)
        p.wait("vector", "dG", dG_export)

        wgu_free = [("tP", 0), ("tP", 0)]
        gsil_free = [0] * 4
        dC_g_last = [0]
        for u in range(4):
            toff = u * C
            vP_h_strip = 0
            for hg in range(6):
                pg = (u * 6 + hg) % 2
                sem, val = wgu_free[pg]
                p.wait("sync", sem, val)
                for d in range(8):
                    p.dma("sync", wgu_sl(pg, d, 0, 512),
                          ccw_out_g[d * 128:(d + 1) * 128, hg * 512:(hg + 1) * 512],
                          "dS")
                    p.dma("sync", wgu_sl(pg, d, 512, 1024),
                          ccw_out_u[d * 128:(d + 1) * 128, hg * 512:(hg + 1) * 512],
                          "dS")
                dS_w = p.c["dS"]
                p.wait("tensor", "dS", dS_w)
                for half in range(2):
                    for d in range(8):
                        for hh in range(half * 2, half * 2 + 2):
                            p.mm(pb(hh), wgu_sl(pg, d, hh * 128, (hh + 1) * 128),
                                 xnT[:, d, toff:toff + C],
                                 start=(d == 0), stop=(d == 7),
                                 inc="tP" if d == 7 else None)
                            p.mm(pb(4 + hh),
                                 wgu_sl(pg, d, 512 + hh * 128, 512 + (hh + 1) * 128),
                                 xnT[:, d, toff:toff + C],
                                 start=(d == 0), stop=(d == 7),
                                 inc="tP" if d == 7 else None)
                    tv = p.c["tP"]
                    for hh in range(half * 2, half * 2 + 2):
                        p.wait("scalar", "tP", tv)
                        p.wait("scalar", "vP", gsil_free[hh])
                        av = p.act(gsil[:, hh, :], pb(hh), AF.Silu, scale=S127,
                                   inc="aP")
                        p.wait("vector", "aP", av)
                        vv = p.stt(hT[:, hg * 4 + hh, :], pb(4 + hh), S127 * S127,
                                   gsil[:, hh, :], ALU.mult, ALU.mult, inc="vP")
                        gsil_free[hh] = vv
                        vP_h_strip = vv
                wgu_free[pg] = ("tP", p.c["tP"])
                if hg < 5:
                    p.wait("tensor", "vP", p.c["vP"])
                else:
                    for hh in range(4):
                        bank_free[hh] = ("aP", p.c["aP"])
                        bank_free[4 + hh] = ("vP", p.c["vP"])
            p.wait("tensor", "vP", vP_h_strip)
            for dvh in range(2):
                p.wait("sync", "cc", cc_w[3])
                if u > 0 or dvh > 0:
                    p.wait("sync", "tP", p.c["tP"])
                for Ht in range(24):
                    p.dma("sync", wdc[:, Ht, :],
                          ccw_out_d[Ht * 128:(Ht + 1) * 128, dvh * 512:(dvh + 1) * 512],
                          "dS")
                dS_wd = p.c["dS"]
                p.wait("tensor", "dS", dS_wd)
                for tt in range(4):
                    k = next_bank()
                    for Ht in range(24):
                        p.mm(pb(k), hT[:, Ht, tt * 128:(tt + 1) * 128], wdc[:, Ht, :],
                             start=(Ht == 0), stop=(Ht == 23),
                             inc="tP" if Ht == 23 else None)
                    tv = p.c["tP"]
                    p.wait("scalar", "tP", tv)
                    if u > 0 and dvh == 0:
                        p.wait("scalar", "dC", dC_g_last[0])
                    av = p.act(g_buf[:, tt, dvh * 512:(dvh + 1) * 512], pb(k),
                               AF.Silu, inc="aP")
                    bank_free[k] = ("aP", av)
            for tt in range(4):
                r0 = toff + tt * 128
                dC_g_last[0] = p.dma("scalar", scr_gg[r0:r0 + 128, :],
                                     g_buf[:, tt, :], "dC")

        dC_p3 = p.c["dC"]
        tP_p3 = p.c["tP"]

        p.wait("gpsimd", "cc", cc_state)
        p.wait("gpsimd", "dC", dC_p3)
        p.wait("vector", "dC", dC_p3)
        p.wait("vector", "tP", tP_p3)
        p.wait("sync", "dC", dC_p3)
        p.wait("sync", "tP", tP_p3)

        dj_free = [("cc", 0), ("cc", 0)]
        for dk in range(8):
            for j in range(8):
                par = (dk * 8 + j) % 2
                sem, val = dj_free[par]
                p.wait("gpsimd", sem, val)
                dg = p.dma("gpsimd", dj[:, par, :],
                           cc_out[j * D + dk * 128:j * D + (dk + 1) * 128, :], "dG")
                p.wait("vector", "dG", dg)
                if j == 0:
                    p.run("vector", lambda h, o=acc[:, dk, :], i=dj[:, par, :],
                          sc=relay[:, 0:1]:
                          h.tensor_scalar(o, i, sc, None, ALU.mult), [("vP", 1)])
                    vv = p.c["vP"]
                else:
                    vv = p.stt(acc[:, dk, :], dj[:, par, :], relay[:, j:j + 1],
                               acc[:, dk, :], ALU.mult, ALU.add, inc="vP")
                dj_free[par] = ("vP", vv)
        vP_acc = p.c["vP"]
        p.wait("tensor", "vP", vP_acc)

        qd_loaded = [0, 0]
        qd_free = [("tP", 0), ("tP", 0)]
        tile_free = {}
        dC_out = [0, 0]
        for grp in range(4):
            gp = grp % 2
            sem, val = qd_free[gp]
            p.wait("sync", sem, val)
            for dk in range(8):
                r0 = (grp * 8 + dk) * 128
                p.dma("sync", qd4_sl(gp, dk, 0, C), scr_qd[r0:r0 + 128, :], "dS")
            qd_loaded[gp] = p.c["dS"]
            for lt in range(4):
                gt = grp * 4 + lt
                par = gt % 2
                sem, val = tile_free.get(("sl", par), ("tP", 0))
                p.wait("sync", sem, val)
                p.dma("sync", sl4[:, par, :], scr_sl[gt * 128:(gt + 1) * 128, :], "dS")
                sem, val = tile_free.get(("g", par), ("tP", 0))
                p.wait("sync", sem, val)
                p.dma("sync", gb4[:, par, :], scr_gg[gt * 128:(gt + 1) * 128, :], "dS")
                sem, val = tile_free.get(("x", par), ("vP", 0))
                p.wait("sync", sem, val)
                dxt = p.dma("sync", x4[:, par, :], x_in[gt * 128:(gt + 1) * 128, :],
                            "dS")
                p.wait("tensor", "dS", max(qd_loaded[gp], dxt))
                tv_last = 0
                for dvh in range(2):
                    k = next_bank()
                    p.mm(pb(k), ident, sl4[:, par, dvh * 512:(dvh + 1) * 512],
                         start=True, stop=False)
                    for d in range(8):
                        p.mm(pb(k), qd4_sl(gp, d, lt * 128, (lt + 1) * 128),
                             acc[:, d, dvh * 512:(dvh + 1) * 512],
                             start=False, stop=(d == 7), inc="tP" if d == 7 else None)
                    tv_last = p.c["tP"]
                    p.wait("vector", "tP", tv_last)
                    p.wait("vector", "dS", dxt)
                    p.wait("vector", "dC", dC_out[par])
                    p.tt(ALU.mult, t24[:, par, :], pb(k),
                         gb4[:, par, dvh * 512:(dvh + 1) * 512])
                    vv = p.tt(ALU.add, o4[:, par, dvh * 512:(dvh + 1) * 512],
                              t24[:, par, :], x4[:, par, dvh * 512:(dvh + 1) * 512],
                              inc="vP")
                    bank_free[k] = ("vP", vv)
                tile_free[("sl", par)] = ("tP", tv_last)
                tile_free[("g", par)] = ("vP", p.c["vP"])
                tile_free[("x", par)] = ("vP", p.c["vP"])
                p.wait("scalar", "vP", p.c["vP"])
                dC_out[par] = p.dma("scalar", out_ext[gt * 128:(gt + 1) * 128, :],
                                    o4[:, par, :], "dC")
            qd_free[gp] = ("tP", tv_last)

        p.wait("scalar", "dC", p.c["dC"])

        if debug:
            p.wait("gpsimd", "dC", p.c["dC"])
            bounce = arb[:, 0:3072]
            for (dst, src) in [(dbg_qd, scr_qd), (dbg_sl, scr_sl),
                               (dbg_gg, scr_gg), (dbg_ccin, cc_in),
                               (dbg_wq, ccw_out_q), (dbg_wd, ccw_out_d)]:
                ncols = src.shape[1]
                for i in range(0, dst.shape[0], 128):
                    dv = p.dma("gpsimd", bounce[:, 0:ncols], src[i:i + 128, :], "dG")
                    p.wait("gpsimd", "dG", dv)
                    dv = p.dma("gpsimd", dst[i:i + 128, :], bounce[:, 0:ncols], "dG")
                    p.wait("gpsimd", "dG", dv)

        def emit(eng_name, h):
            for item in p.ev[eng_name]:
                if item[0] == "w":
                    h.wait_ge(sems[item[1]], item[2])
                else:
                    ins = item[1](h)
                    for sname, n in item[2]:
                        ins = ins.then_inc(sems[sname], n)

        @block.tensor
        def _(t):
            emit("tensor", t)

        @block.vector
        def _(vv_):
            emit("vector", vv_)

        @block.scalar
        def _(sc_):
            emit("scalar", sc_)

        @block.sync
        def _(sy_):
            emit("sync", sy_)

        @block.gpsimd
        def _(gp_):
            emit("gpsimd", gp_)

    return nc


def make_inmaps(x, wqkv, bq, wg, wu, wd):
    t = np.arange(TCORE, dtype=np.float64)
    tc = np.mod(t, C)
    qdec = np.broadcast_to((ALPHA ** (tc + 1) * INVSQ).astype(np.float32),
                           (128, TCORE)).copy()
    kdec = np.broadcast_to((ALPHA ** (-(tc + 1))).astype(np.float32),
                           (128, TCORE)).copy()
    kpp = np.zeros((128, NGT), dtype=np.float32)
    for gt in range(NGT):
        jl = np.mod(gt * 128 + np.arange(128, dtype=np.float64), C)
        kpp[:, gt] = (ALPHA ** (C - 1 - jl)).astype(np.float32)
    bpp = np.ascontiguousarray(bq.reshape(24, 128).T)
    bbc = np.ascontiguousarray(np.broadcast_to(bq[D:3 * D], (128, 2 * D)))
    maskm = np.zeros((128, 4, C), dtype=np.float32)
    pidx = np.arange(128)
    for b in range(4):
        maskm[:, b, :] = (np.arange(C)[None, :] >= (b * 128 + pidx)[:, None])
    mask_bf = np.ascontiguousarray(maskm.reshape(128, 4 * C).astype(bf))
    ident = np.eye(128, dtype=np.float32).astype(bf)

    def blockify(a, w):
        return np.ascontiguousarray(
            a.reshape(3072 // 128, 128, w).transpose(1, 0, 2).reshape(128, 3072))

    in_maps = []
    for c in range(NCORES):
        relay = np.zeros((128, 8), dtype=np.float32)
        for j in range(c):
            relay[:, j] = np.float32(ALPHA ** (2048.0 * (c - 1 - j)))
        wd_sl = wd[:, c * 384:(c + 1) * 384]
        wd_blk = np.ascontiguousarray(
            wd_sl.reshape(8, 128, 384).transpose(1, 0, 2).reshape(128, 3072))
        in_maps.append({
            "x_in": x[c * TCORE:(c + 1) * TCORE],
            "wq_sl": blockify(wqkv[:, c * 128:(c + 1) * 128], 128),
            "wg_sl": blockify(wg[:, c * 128:(c + 1) * 128], 128),
            "wu_sl": blockify(wu[:, c * 128:(c + 1) * 128], 128),
            "wd_sl": wd_blk,
            "bpp_in": bpp, "bbc_in": bbc, "qdec_in": qdec, "kdec_in": kdec,
            "kpp_in": kpp, "relay_in": relay, "mask_in": mask_bf,
            "ident_in": ident,
        })
    return in_maps


def kernel(L0_attn_x, L0_attn_W_qkv, L0_attn_b_qkv, L0_ffn_w_gate, L0_ffn_w_up,
           L0_ffn_w_down):
    x = np.ascontiguousarray(np.asarray(L0_attn_x, dtype=np.float32).reshape(16384, D))
    wqkv = np.ascontiguousarray(np.asarray(L0_attn_W_qkv, dtype=np.float32))
    bq = np.asarray(L0_attn_b_qkv, dtype=np.float32)
    wg = np.ascontiguousarray(np.asarray(L0_ffn_w_gate, dtype=np.int32))
    wu = np.ascontiguousarray(np.asarray(L0_ffn_w_up, dtype=np.int32))
    wd = np.ascontiguousarray(np.asarray(L0_ffn_w_down, dtype=np.int32))

    nc = build_nc()
    in_maps = make_inmaps(x, wqkv, bq, wg, wu, wd)
    res = run_bass_kernel_spmd(nc, in_maps, core_ids=list(range(NCORES)))
    out = np.concatenate([np.asarray(r["out_ext"]) for r in res.results], axis=0)
    return out.reshape(1, 16384, D).astype(np.float32)
